# version 01
# baseline (speedup 1.0000x reference)
import math
import numpy as np
import ml_dtypes
import concourse.bass as bass
import concourse.mybir as mybir
from concourse.alu_op_type import AluOpType as ALU
from concourse.bass_core_rust import IndirectOffsetOnAxis
from contextlib import ExitStack

F32 = mybir.dt.float32
BF16 = mybir.dt.bfloat16
I32 = mybir.dt.int32
U32 = mybir.dt.uint32
AF = mybir.ActivationFunctionType
AX = mybir.AxisListType

B, S, D = 4, 4096, 2048
T = B * S
HEADS, HD = 16, 128
HPC = 8
E = 8
HF = 2816
NHT = HF // 128
KVR = 256
CAP = 2048.0
GPAD = 640
WIN = 4 * GPAD
NDM = D // 128
NTT = 16
NPT = 32
OOBV = float(1 << 28)
ISQ = 1.0 / math.sqrt(128.0)

CVALS = [1e-5, float(KVR) * 1e-6, -0.5, 0.5 * math.log(float(KVR)),
         math.log(ISQ), ISQ, -1.0, 0.5, 1.0 / float(D), float(GPAD), OOBV,
         2048.0, 2.0]

bfd = ml_dtypes.bfloat16


class Prog:

    def __init__(self):
        self.nc = bass.Bass("TRN2", target_bir_lowering=False)
        self.ctx = ExitStack()
        self._sems = {}
        self._cnt = {}
        self.engs = {}

    _uid = 0

    def _uniq(self, name):
        Prog._uid += 1
        return f"{name}_{Prog._uid}"

    def sb(self, stack, name, shape, dtype):
        return stack.enter_context(self.nc.sbuf_tensor(self._uniq(name), shape, dtype))

    def psumt(self, stack, name, shape):
        return stack.enter_context(self.nc.psum_tensor(self._uniq(name), shape, F32))

    def dram(self, name, shape, dtype, kind="Internal", addr_space="Local"):
        return self.nc.dram_tensor(name, shape, dtype, kind=kind,
                                   addr_space=addr_space).ap()

    def open(self):
        for nm in ("v", "s", "p", "t", "y"):
            self._sems[nm] = self.ctx.enter_context(self.nc.semaphore("ch_" + nm))
            self._cnt[nm] = 0
        self.block = self.ctx.enter_context(self.nc.Block())
        self.engs = {"v": self.nc.vector, "s": self.nc.scalar,
                     "p": self.nc.gpsimd, "t": self.nc.tensor,
                     "y": self.nc.sync}

    def track(self, ename, instr, k):
        instr.then_inc(self._sems[ename], k)
        self._cnt[ename] += k
        return (ename, self._cnt[ename])

    def op(self, ename, instr):
        return self.track(ename, instr, 1)

    def dma(self, ename, instr):
        return self.track(ename, instr, 16)

    def bump(self, ename, k):
        self._cnt[ename] += k
        return (ename, self._cnt[ename])

    def wait(self, ename, *tickets):
        need = {}
        for tk in tickets:
            if tk is None:
                continue
            src, cnt = tk
            need[src] = max(need.get(src, 0), cnt)
        for src, cnt in need.items():
            self.engs[ename].wait_ge(self._sems[src], cnt)

    def mark(self):
        return dict(self._cnt)

    def wait_mark(self, ename, mk):
        for src, cnt in mk.items():
            if cnt > 0:
                self.engs[ename].wait_ge(self._sems[src], cnt)

    def barrier(self):
        mk = self.mark()
        for en in self.engs:
            self.wait_mark(en, mk)

    def close(self):
        mk = self.mark()
        self.wait_mark("y", mk)
        self.ctx.close()


def rope_perm():
    return np.concatenate([np.arange(0, HD, 2), np.arange(1, HD, 2)])


def host_consts():
    c = {}
    c["triu128"] = np.ascontiguousarray(
        np.triu(np.ones((128, 128), np.float32), k=1).astype(bfd))
    c["ones128"] = np.ones((128, 1), np.float32).astype(bfd)
    c["ones128f"] = np.ones((128, 1), np.float32)
    c["ones1x128f"] = np.ones((1, 128), np.float32)
    masks = []
    for dlt in (0, 128, 256, 384):
        kk = np.arange(128)[:, None] + dlt
        qq = np.arange(512)[None, :]
        masks.append((kk <= qq).astype(np.float32))
    c["masks"] = np.ascontiguousarray(np.concatenate(masks, 1).astype(bfd))
    inv = 1.0 / (10000.0 ** (np.arange(0, HD, 2, dtype=np.float32) / HD))
    ang = np.arange(S, dtype=np.float32)[None, :] * inv[:, None]
    c["ropecos"] = np.ascontiguousarray(np.cos(ang).astype(np.float32))
    c["ropesin"] = np.ascontiguousarray(np.sin(ang).astype(np.float32))
    return c


def ph_ln(p, st, src_dram, n_tiles, hi_dram, lo_dram=None, attout_dram=None,
          x1_dram=None, bob_sb=None, x_src_rows0=0):
    nc = p.nc
    with ExitStack() as sk:
        xin = [p.sb(sk, f"lnx{i}", [128, D], F32) for i in range(2)]
        att = p.sb(sk, "lnatt", [128, D], BF16)
        sq = p.sb(sk, "lnsq", [128, D], BF16)
        s1 = p.sb(sk, "lns1", [128, 1], F32)
        s2 = p.sb(sk, "lns2", [128, 1], F32)
        s3 = p.sb(sk, "lns3", [128, 1], F32)
        s4 = p.sb(sk, "lns4", [128, 1], F32)
        s5 = p.sb(sk, "lns5", [128, 1], F32)
        hi = p.sb(sk, "lnhi", [128, D], BF16)
        hif = p.sb(sk, "lnhif", [128, D], F32)
        lo = p.sb(sk, "lnlo", [128, D], BF16)
        free = [[], []]
        for i in range(n_tiles):
            pa = i % 2
            p.wait("y", *free[pa])
            tin = p.dma("y", nc.sync.dma_start(
                out=xin[pa][:],
                in_=src_dram[x_src_rows0 + i * 128: x_src_rows0 + (i + 1) * 128, :]))
            if attout_dram is not None:
                tat = None
                for dc in range(4):
                    tat = p.dma("y", nc.sync.dma_start(
                        out=att[:, dc * 512:(dc + 1) * 512],
                        in_=attout_dram[dc * 512:(dc + 1) * 512,
                                        i * 128:(i + 1) * 128],
                        transpose=True))
                p.wait("v", tin, tat)
                tv = p.op("v", nc.vector.tensor_add(xin[pa][:], xin[pa][:], att[:]))
                if bob_sb is not None:
                    p.wait("v", tv)
                    tv = p.op("v", nc.vector.tensor_add(xin[pa][:], xin[pa][:],
                                                        bob_sb[:]))
                p.wait("y", tv)
                tx1s = p.dma("y", nc.sync.dma_start(
                    out=x1_dram[i * 128:(i + 1) * 128, :], in_=xin[pa][:]))
                extra_free = [tx1s]
                tin = tv
            else:
                extra_free = []
            p.wait("v", tin)
            tv1 = p.op("v", nc.vector.reduce_sum(s1[:, 0:1], xin[pa][:], axis=AX.X))
            p.wait("s", tin)
            ts1 = p.op("s", nc.scalar.activation(sq[:], xin[pa][:], AF.Square,
                                                 accum_out=s2[:, 0:1]))
            p.wait("v", tv1)
            tv2 = p.op("v", nc.vector.tensor_scalar_mul(s3[:, 0:1], s1[:, 0:1],
                                                        1.0 / D))
            p.wait("v", tv2)
            tv3 = p.op("v", nc.vector.tensor_mul(s5[:, 0:1], s3[:, 0:1], s3[:, 0:1]))
            p.wait("v", tv3, ts1)
            tv4 = p.op("v", nc.vector.tensor_scalar_mul(s4[:, 0:1], s2[:, 0:1],
                                                        1.0 / D))
            p.wait("v", tv4)
            tv5 = p.op("v", nc.vector.tensor_sub(s4[:, 0:1], s4[:, 0:1], s5[:, 0:1]))
            p.wait("s", tv5)
            ts2 = p.op("s", nc.scalar.activation(s5[:, 0:1], s4[:, 0:1], AF.Ln,
                                                 bias=1e-5))
            p.wait("s", ts2)
            ts3 = p.op("s", nc.scalar.activation(s4[:, 0:1], s5[:, 0:1], AF.Exp,
                                                 scale=-0.5))
            p.wait("v", ts3)
            tv6 = p.op("v", nc.vector.tensor_scalar(hi[:], xin[pa][:], s3[:, 0:1],
                                                    s4[:, 0:1], ALU.subtract,
                                                    ALU.mult))
            p.wait("y", tv6)
            p.dma("y", nc.sync.dma_start(
                out=hi_dram[x_src_rows0 + i * 128: x_src_rows0 + (i + 1) * 128, :],
                in_=hi[:]))
            last = tv6
            if lo_dram is not None:
                tv7 = p.op("v", nc.vector.tensor_scalar(hif[:], xin[pa][:],
                                                        s3[:, 0:1], s4[:, 0:1],
                                                        ALU.subtract, ALU.mult))
                p.wait("v", tv7, tv6)
                tv8 = p.op("v", nc.vector.tensor_sub(lo[:], hif[:], hi[:]))
                p.wait("y", tv8)
                p.dma("y", nc.sync.dma_start(
                    out=lo_dram[x_src_rows0 + i * 128: x_src_rows0 + (i + 1) * 128, :],
                    in_=lo[:]))
                last = tv8
            free[pa] = extra_free + [last]
        p.barrier()


def load_xlnT_chunk(p, xt_sb, hi_dram, pos0, width=512):
    nc = p.nc
    tk = None
    for t in range(NDM):
        tk = p.dma("y", nc.sync.dma_start(
            out=xt_sb[:, t * width:(t + 1) * width],
            in_=hi_dram[pos0:pos0 + width, t * 128:(t + 1) * 128],
            transpose=True))
    return tk


def ph_proj_feat(p, st, hi_dram, npos, w_sb, w_cols, nouts, out_dram,
                 rope=False, cosf=None, sinf=None, kacc_sb=None, kacc_tiles=1,
                 kacc_cols=None):
    nc = p.nc
    with ExitStack() as sk:
        xt = [p.sb(sk, f"pjx{i}", [128, NDM * 512], BF16) for i in range(2)]
        ev = [p.sb(sk, f"pjev{i}", [128, 512], BF16) for i in range(2)]
        r1 = p.sb(sk, "pjr1", [64, 512], F32)
        r2 = p.sb(sk, "pjr2", [64, 512], F32)
        r3 = p.sb(sk, "pjr3", [64, 512], F32)
        r4 = p.sb(sk, "pjr4", [64, 512], F32)
        ps = [p.psumt(sk, f"pjp{i}", [128, 512]) for i in range(2)]
        free = [None, None]
        psfree = [None, None]
        evfree = [None, None]
        nch = npos // 512
        for pc in range(nch):
            pa = pc % 2
            if kacc_sb is None:
                p.wait("y", free[pa])
                tld = load_xlnT_chunk(p, xt[pa], hi_dram, pc * 512)
                src_sb, ktiles, kcols = xt[pa], NDM, 512
            else:
                tld = None
                src_sb, ktiles, kcols = kacc_sb, kacc_tiles, kacc_cols
            p.wait("t", tld)
            lastmm = None
            for o in range(nouts):
                pb = o % 2
                p.wait("t", psfree[pb])
                for kt in range(ktiles):
                    lastmm = p.op("t", nc.tensor.matmul(
                        ps[pb][:, :],
                        w_sb[:, kt * w_cols + o * 128: kt * w_cols + (o + 1) * 128],
                        src_sb[:, kt * kcols + (pc * 512 if kacc_sb is not None else 0):
                               kt * kcols + (pc * 512 if kacc_sb is not None else 0) + 512],
                        start=(kt == 0), stop=(kt == ktiles - 1),
                        skip_group_check=True))
                eb = ev[o % 2]
                p.wait("v", lastmm, evfree[o % 2])
                if not rope:
                    tv = p.op("v", nc.vector.tensor_copy(eb[:], ps[pb][:, :]))
                else:
                    cs = cosf[:, pc * 512:(pc + 1) * 512]
                    sn = sinf[:, pc * 512:(pc + 1) * 512]
                    ta = p.op("v", nc.vector.tensor_mul(r1[:], ps[pb][0:64, :], cs))
                    tb = p.op("v", nc.vector.tensor_mul(r2[:], ps[pb][64:128, :], sn))
                    tc = p.op("v", nc.vector.tensor_mul(r3[:], ps[pb][0:64, :], sn))
                    td = p.op("v", nc.vector.tensor_mul(r4[:], ps[pb][64:128, :], cs))
                    p.wait("v", ta, tb)
                    t1 = p.op("v", nc.vector.tensor_sub(eb[0:64, :], r1[:], r2[:]))
                    p.wait("v", tc, td)
                    tv = p.op("v", nc.vector.tensor_add(eb[64:128, :], r3[:], r4[:]))
                psfree[pb] = tv
                p.wait("y", tv)
                tw = p.dma("y", nc.sync.dma_start(
                    out=out_dram[o * 128:(o + 1) * 128, pc * 512:(pc + 1) * 512],
                    in_=eb[:]))
                evfree[o % 2] = tw
                free[pa] = tw
        p.barrier()


def ph_proj_tok(p, st, hi_dram, npos, rhs_sb, rhs_cols_list, out_writer,
                kacc=None):
    nc = p.nc
    with ExitStack() as sk:
        xt = [p.sb(sk, f"pmx{i}", [128, NDM * 512], BF16) for i in range(2)]
        ps = [p.psumt(sk, f"pmp{i}", [128, 512]) for i in range(2)]
        free = [None, None]
        psfree = [None, None]
        rhs_w = sum(w for _, w in rhs_cols_list)
        nch = npos // 512
        for pc in range(nch):
            pa = pc % 2
            if kacc is None:
                p.wait("y", free[pa])
                tld = load_xlnT_chunk(p, xt[pa], hi_dram, pc * 512)
                src_sb, ktiles, kcols = xt[pa], NDM, 512
                p.wait("t", tld)
            else:
                src_sb, ktiles, kcols = kacc
            lastv = None
            for pt in range(4):
                for ci, (c0, w) in enumerate(rhs_cols_list):
                    pb = (pt * len(rhs_cols_list) + ci) % 2
                    p.wait("t", psfree[pb])
                    lastmm = None
                    for kt in range(ktiles):
                        off = kt * kcols + (pc * 512 if kacc is not None else 0)
                        lastmm = p.op("t", nc.tensor.matmul(
                            ps[pb][:, 0:w],
                            src_sb[:, off + pt * 128: off + (pt + 1) * 128],
                            rhs_sb[:, kt * rhs_w + c0: kt * rhs_w + c0 + w],
                            start=(kt == 0), stop=(kt == ktiles - 1),
                            skip_group_check=True))
                    tv = out_writer(pc * 4 + pt, ci, ps[pb], lastmm)
                    psfree[pb] = tv
                    lastv = tv
            free[pa] = lastv
        p.barrier()


def ph_pair_barrier(p, st, rnd):
    nc = p.nc
    g = nc.gpsimd
    flags_sh = st["flags_sh"]
    salts = st["salt_sb"]
    flc = st["flagc"]
    p.barrier()
    for arm in g.switch_core_id(n=128):
        half = (arm.logical % 8) % 2
        g.dma_start(out=flags_sh[half:half + 1, rnd:rnd + 1],
                    in_=salts[0:1, rnd:rnd + 1]).then_inc(p._sems["p"], 16)
    twr = p.bump("p", 16)
    p.wait("p", twr)
    r_cont, r_tgt, r_a, r_b, r_exp = (st["r_cont"], st["r_tgt"], st["r_a"],
                                      st["r_b"], st["r_exp"])
    g.reg_load(r_exp, salts[0:1, rnd:rnd + 1])
    g.reg_mov(r_cont, 1)
    with g.While(r_cont):
        g.dma_start(out=flc[:, 0:1], in_=flags_sh[:, rnd:rnd + 1],
                    allow_slow_non_contiguous=True).then_inc(st["pollsem"], 16)
        g.reg_add(r_tgt, r_tgt, 16)
        g.wait_ge(st["pollsem"], r_tgt)
        g.reg_load(r_a, flc[0:1, 0:1])
        g.reg_load(r_b, flc[1:2, 0:1])
        g.reg_alu(r_a, r_a, r_exp, ALU.is_equal)
        g.reg_alu(r_b, r_b, r_exp, ALU.is_equal)
        g.reg_alu(r_cont, r_a, r_b, ALU.bitwise_and)
        g.reg_alu(r_cont, r_cont, 1, ALU.bitwise_xor)
    tp = p.op("p", g.memset(st["flagv"][0:1, 0:1], 0))
    for en in ("v", "s", "t", "y"):
        p.wait(en, tp)


def setup_poll_state(p, stack, flags_sh, salt_sb):
    g = p.nc.gpsimd
    st = {}
    st["flags_sh"] = flags_sh
    st["salt_sb"] = salt_sb
    st["flagv"] = p.sb(stack, "flagv", [1, 1], I32)
    st["flagc"] = p.sb(stack, "flagc", [2, 1], I32)
    st["pollsem"] = stack.enter_context(p.nc.semaphore("pollsem"))
    st["r_cont"] = stack.enter_context(g.register("r_cont"))
    st["r_tgt"] = stack.enter_context(g.register("r_tgt"))
    st["r_a"] = stack.enter_context(g.register("r_a"))
    st["r_b"] = stack.enter_context(g.register("r_b"))
    st["r_exp"] = stack.enter_context(g.register("r_exp"))
    g.reg_mov(st["r_tgt"], 0)
    return st


def ph_attention(p, st, qt_dram, kt_dram, vp_dram, ctx8_dram, npos, scale,
                 ctx_writer=None):
    nc = p.nc
    nq = npos // 512
    with ExitStack() as sk:
        qth = [p.sb(sk, f"atq{i}", [128, npos], BF16) for i in range(2)]
        kth = [p.sb(sk, f"atk{i}", [128, npos], BF16) for i in range(2)]
        vph = [p.sb(sk, f"atv{i}", [128, 2 * npos], BF16) for i in range(2)]
        pt_ = [p.sb(sk, f"atp{i}", [128, 512], BF16) for i in range(2)]
        sqd = p.sb(sk, "atsq", [128, 512], BF16)
        qn = p.sb(sk, "atqn", [1, npos], F32)
        kn = p.sb(sk, "atkn", [1, npos], F32)
        m1 = p.sb(sk, "atm1", [1, 1], F32)
        m2 = p.sb(sk, "atm2", [1, 1], F32)
        bias = p.sb(sk, "atbias", [128, 1], F32)
        rcp = p.sb(sk, "atrcp", [1, 512], F32)
        bcs = p.sb(sk, "atbcs", [128, 512], F32)
        ctxs = [p.sb(sk, f"atc{i}", [128, npos], BF16) for i in range(2)]
        masks = st["masks_sb"]
        ones = st["ones_sb"]
        ones1x = st["ones1x_sb"]
        pst = [p.psumt(sk, f"apS{i}", [128, 512]) for i in range(2)]
        pctx = p.psumt(sk, "apC", [128, 512])
        pden = p.psumt(sk, "apD", [1, 512])
        paux = p.psumt(sk, "apX", [128, 512])
        hfree = [None, None]
        ctxfree = [None, None]
        prev_tc = None
        for h in range(HPC):
            hb = h % 2
            p.wait("y", hfree[hb])
            tq = p.dma("y", nc.sync.dma_start(
                out=qth[hb][:], in_=qt_dram[h * 128:(h + 1) * 128, :]))
            tk = p.dma("y", nc.sync.dma_start(
                out=kth[hb][:], in_=kt_dram[h * 128:(h + 1) * 128, :]))
            tv_ = None
            if h % 2 == 0:
                p.wait("y", hfree[0], hfree[1])
                hp = h // 2
                tv_ = p.dma("y", nc.sync.dma_start(
                    out=vph[(h // 2) % 2][:].rearrange("p (t c) -> p t c", c=256),
                    in_=vp_dram.rearrange("(g t p) c -> g p t c", g=4, p=128)[hp]))
            voff = (h % 2) * 128
            vb = (h // 2) % 2
            tqn = tkn = None
            p.wait("t", prev_tc)
            for c in range(nq):
                p.wait("s", tq if c == 0 else None)
                tsq = p.op("s", nc.scalar.activation(
                    sqd[:], qth[hb][:, c * 512:(c + 1) * 512], AF.Square))
                p.wait("t", tsq)
                tmm = p.op("t", nc.tensor.matmul(pden[:, :], ones[:], sqd[:],
                                                 start=True, stop=True,
                                                 skip_group_check=True))
                p.wait("s", tmm)
                tqn = p.op("s", nc.scalar.copy(qn[0:1, c * 512:(c + 1) * 512],
                                               pden[0:1, :]))
            for c in range(nq):
                p.wait("s", tk if c == 0 else None)
                tsq = p.op("s", nc.scalar.activation(
                    sqd[:], kth[hb][:, c * 512:(c + 1) * 512], AF.Square))
                p.wait("t", tsq)
                tmm = p.op("t", nc.tensor.matmul(pden[:, :], ones[:], sqd[:],
                                                 start=True, stop=True,
                                                 skip_group_check=True))
                p.wait("s", tmm)
                tkn = p.op("s", nc.scalar.copy(kn[0:1, c * 512:(c + 1) * 512],
                                               pden[0:1, :]))
            p.wait("v", tqn, tkn)
            tv1 = p.op("v", nc.vector.reduce_max(m1[0:1, 0:1], qn[0:1, :], axis=AX.X))
            tv2 = p.op("v", nc.vector.reduce_max(m2[0:1, 0:1], kn[0:1, :], axis=AX.X))
            p.wait("v", tv1, tv2)
            tv3 = p.op("v", nc.vector.tensor_mul(m1[0:1, 0:1], m1[0:1, 0:1],
                                                 m2[0:1, 0:1]))
            p.wait("s", tv3)
            ts1 = p.op("s", nc.scalar.activation(m2[0:1, 0:1], m1[0:1, 0:1], AF.Ln))
            p.wait("s", ts1)
            ts2 = p.op("s", nc.scalar.activation(m1[0:1, 0:1], m2[0:1, 0:1], AF.Exp,
                                                 scale=0.5, bias=math.log(scale)))
            p.wait("v", ts2)
            tv4 = p.op("v", nc.vector.tensor_scalar_mul(m2[0:1, 0:1], m1[0:1, 0:1],
                                                        -1.0))
            p.wait("t", tv4)
            tmm = p.op("t", nc.tensor.matmul(paux[:, 0:1], ones1x[:], m2[0:1, 0:1],
                                             start=True, stop=True,
                                             skip_group_check=True))
            p.wait("s", tmm)
            tbias = p.op("s", nc.scalar.copy(bias[:, 0:1], paux[:, 0:1]))
            stfree = [None, None]
            pfree = [None, None]
            for qc in range(nq):
                nkt = (qc + 1) * 4
                lastden = lastctx = None
                for kt in range(nkt):
                    sb_ = kt % 2
                    p.wait("t", stfree[sb_], tbias if (qc == 0 and kt == 0) else None,
                           tq if (qc == 0 and kt == 0) else None,
                           tk if (qc == 0 and kt == 0) else None,
                           tv_ if (qc == 0 and kt == 0 and h % 2 == 0) else None)
                    tS = p.op("t", nc.tensor.matmul(
                        pst[sb_][:, :],
                        kth[hb][:, kt * 128:(kt + 1) * 128],
                        qth[hb][:, qc * 512:(qc + 1) * 512],
                        start=True, stop=True, skip_group_check=True))
                    p.wait("s", tS, pfree[sb_],
                           tbias if (qc == 0 and kt == 0) else None)
                    tE = p.op("s", nc.scalar.activation(
                        pt_[sb_][:], pst[sb_][:, :], AF.Exp,
                        bias=bias[:, 0:1], scale=scale))
                    tP = tE
                    if kt >= qc * 4:
                        dlt = (kt - qc * 4) * 128
                        p.wait("v", tE)
                        case = dlt // 128
                        tP = p.op("v", nc.vector.tensor_mul(
                            pt_[sb_][:], pt_[sb_][:],
                            masks[:, case * 512:(case + 1) * 512]))
                    stfree[sb_] = tP
                    p.wait("t", tP)
                    lastden = p.op("t", nc.tensor.matmul(
                        pden[:, :], ones[:], pt_[sb_][:],
                        start=(kt == 0), stop=(kt == nkt - 1),
                        skip_group_check=True))
                    lastctx = p.op("t", nc.tensor.matmul(
                        pctx[:, :],
                        vph[vb][:, kt * 256 + voff: kt * 256 + voff + 128],
                        pt_[sb_][:],
                        start=(kt == 0), stop=(kt == nkt - 1),
                        skip_group_check=True))
                    pfree[sb_] = lastctx
                p.wait("v", lastden)
                tr = p.op("v", nc.vector.reciprocal(rcp[0:1, :], pden[0:1, :]))
                p.wait("t", tr)
                tb2 = p.op("t", nc.tensor.matmul(paux[:, :], ones1x[:], rcp[0:1, :],
                                                 start=True, stop=True,
                                                 skip_group_check=True))
                p.wait("s", tb2)
                tb3 = p.op("s", nc.scalar.copy(bcs[:], paux[:, :]))
                p.wait("v", lastctx, tb3, ctxfree[hb])
                tc_ = p.op("v", nc.vector.tensor_mul(
                    ctxs[hb][:, qc * 512:(qc + 1) * 512], pctx[:, :], bcs[:]))
                stfree = [tc_, tc_]
                pfree = [tc_, tc_]
            prev_tc = tc_
            p.wait("y", tc_)
            if ctx_writer is None:
                tw = p.dma("y", nc.sync.dma_start(
                    out=ctx8_dram[h * 128:(h + 1) * 128, :], in_=ctxs[hb][:]))
            else:
                tw = ctx_writer(h, ctxs[hb])
            ctxfree[hb] = tw
            hfree[hb] = tw
        p.barrier()


def ph_wo(p, st, ctxm_dram, wo_dram_in, outT_dram, ntok):
    nc = p.nc
    with ExitStack() as sk:
        wo = p.sb(sk, "wosb", [128, 16 * D], BF16)
        cx = [p.sb(sk, f"wocx{i}", [128, 16 * 512], BF16) for i in range(2)]
        ev = [p.sb(sk, f"woev{i}", [128, 512], BF16) for i in range(2)]
        ps = [p.psumt(sk, f"wops{i}", [128, 512]) for i in range(2)]
        tw = None
        for at in range(16):
            tw = p.dma("p", nc.gpsimd.dma_start(
                out=wo[:, at * D:(at + 1) * D],
                in_=wo_dram_in[at * 128:(at + 1) * 128, :]))
        free = [None, None]
        psfree = [None, None]
        evfree = [None, None]
        for tc in range(ntok // 512):
            pa = tc % 2
            p.wait("y", free[pa])
            tld = None
            for at in range(16):
                tld = p.dma("y", nc.sync.dma_start(
                    out=cx[pa][:, at * 512:(at + 1) * 512],
                    in_=ctxm_dram[at * 128:(at + 1) * 128, tc * 512:(tc + 1) * 512]))
            p.wait("t", tld, tw if tc == 0 else None)
            lastv = None
            for dc in range(16):
                pb = dc % 2
                p.wait("t", psfree[pb])
                lastmm = None
                for at in range(16):
                    lastmm = p.op("t", nc.tensor.matmul(
                        ps[pb][:, :],
                        wo[:, at * D + dc * 128: at * D + (dc + 1) * 128],
                        cx[pa][:, at * 512:(at + 1) * 512],
                        start=(at == 0), stop=(at == 15),
                        skip_group_check=True))
                p.wait("v", lastmm, evfree[pb])
                tv = p.op("v", nc.vector.tensor_copy(ev[pb][:], ps[pb][:, :]))
                psfree[pb] = tv
                p.wait("y", tv)
                td = p.dma("y", nc.sync.dma_start(
                    out=outT_dram[dc * 128:(dc + 1) * 128, tc * 512:(tc + 1) * 512],
                    in_=ev[pb][:]))
                evfree[pb] = td
                lastv = td
            free[pa] = lastv
        p.barrier()


def ph_router(p, st, hi_dram, lo_dram, wg_hi, wg_lo, meta_eg_sb, ntok):
    nc = p.nc
    with ExitStack() as sk:
        xt = p.sb(sk, "rtx", [128, NDM * 512], BF16)
        xl = p.sb(sk, "rtxl", [128, NDM * 512], BF16)
        lg = p.sb(sk, "rtlg", [128, 8], F32)
        mx8 = p.sb(sk, "rtmx", [128, 8], F32)
        mi8 = p.sb(sk, "rtmi", [128, 8], U32)
        mif = p.sb(sk, "rtmif", [128, 1], F32)
        nmx = p.sb(sk, "rtnmx", [128, 1], F32)
        sme = p.sb(sk, "rtsme", [128, 1], F32)
        dmp = p.sb(sk, "rtdmp", [128, 8], BF16)
        ps = p.psumt(sk, "rtps", [128, 8])
        prev = None
        psfree = None
        lastmm_prev = None
        for pc in range(ntok // 512):
            p.wait("y", lastmm_prev)
            t1 = load_xlnT_chunk(p, xt, hi_dram, pc * 512)
            t2 = load_xlnT_chunk(p, xl, lo_dram, pc * 512)
            p.wait("t", t1, t2)
            for pt in range(4):
                tt = pc * 4 + pt
                lastmm = None
                p.wait("t", psfree)
                for kt in range(NDM):
                    co = kt * 512 + pt * 128
                    lastmm = p.op("t", nc.tensor.matmul(
                        ps[:, :], xt[:, co:co + 128], wg_hi[:, kt * 8:(kt + 1) * 8],
                        start=(kt == 0), stop=False, skip_group_check=True))
                    lastmm = p.op("t", nc.tensor.matmul(
                        ps[:, :], xt[:, co:co + 128], wg_lo[:, kt * 8:(kt + 1) * 8],
                        start=False, stop=False, skip_group_check=True))
                    lastmm = p.op("t", nc.tensor.matmul(
                        ps[:, :], xl[:, co:co + 128], wg_hi[:, kt * 8:(kt + 1) * 8],
                        start=False, stop=(kt == NDM - 1), skip_group_check=True))
                p.wait("v", lastmm, prev)
                tv0 = p.op("v", nc.vector.tensor_copy(lg[:], ps[:, :]))
                psfree_t = tv0
                p.wait("v", tv0)
                tv1 = p.op("v", nc.vector.max(mx8[:, :], lg[:, :]))
                p.wait("v", tv1)
                tv2 = p.op("v", nc.vector.max_index(mi8[:, :], mx8[:, :], lg[:, :]))
                p.wait("v", tv2)
                tv3 = p.op("v", nc.vector.tensor_copy(mif[:, 0:1], mi8[:, 0:1]))
                tv4 = p.op("v", nc.vector.tensor_scalar_mul(nmx[:, 0:1],
                                                            mx8[:, 0:1], -1.0))
                p.wait("s", tv4, tv0)
                ts1 = p.op("s", nc.scalar.activation(dmp[:, :], lg[:, :], AF.Exp,
                                                     bias=nmx[:, 0:1],
                                                     accum_out=sme[:, 0:1]))
                p.wait("v", ts1, tv3)
                tv5 = p.op("v", nc.vector.reciprocal(sme[:, 0:1], sme[:, 0:1]))
                p.wait("v", tv5)
                tv6 = p.op("v", nc.vector.tensor_copy(
                    meta_eg_sb[:, 2 * tt:2 * tt + 1], mif[:, 0:1]))
                tv7 = p.op("v", nc.vector.tensor_copy(
                    meta_eg_sb[:, 2 * tt + 1:2 * tt + 2], sme[:, 0:1]))
                prev = tv7
                psfree = psfree_t
                lastmm_prev = lastmm
        p.barrier()


def ph_compact(p, st, meta_pair_sb, slot_sb, ntiles, base_init_sb=None,
               gate_keep=None, col_of=None, tile0=0):
    nc = p.nc
    with ExitStack() as sk:
        oh = p.sb(sk, "cpoh", [128, 8], BF16)
        ohf = p.sb(sk, "cpohf", [128, 8], F32)
        rb = p.sb(sk, "cprb", [1, 8], F32)
        tmp = p.sb(sk, "cptmp", [128, 8], F32)
        lp = p.sb(sk, "cplp", [128, 1], F32)
        ov = p.sb(sk, "cpov", [128, 1], F32)
        pcum = p.psumt(sk, "cppc", [128, 8])
        ptot = p.psumt(sk, "cppt", [1, 8])
        triu = st["triu_sb"]
        ones = st["ones_sb"]
        onesf1x = st["ones1x_sb"]
        if base_init_sb is None:
            tz = p.op("v", nc.vector.memset(rb[:], 0.0))
        else:
            tz = p.op("v", nc.vector.tensor_copy(rb[:], base_init_sb[:]))
        prev = tz
        for ti in range(ntiles):
            t = tile0 + ti
            c0 = col_of(t) if col_of is not None else 2 * t
            eidx = meta_pair_sb[:, c0:c0 + 1]
            toh = None
            p.wait("v", prev)
            for e in range(E):
                toh = p.op("v", nc.vector.tensor_single_scalar(
                    oh[:, e:e + 1], eidx, float(e), ALU.is_equal))
            p.wait("t", toh, prev)
            tm1 = p.op("t", nc.tensor.matmul(pcum[:, :], triu[:], oh[:, :],
                                             start=True, stop=False,
                                             skip_group_check=True))
            tm1 = p.op("t", nc.tensor.matmul(pcum[:, :], onesf1x[:], rb[0:1, :],
                                             start=False, stop=True,
                                             skip_group_check=True))
            tm2 = p.op("t", nc.tensor.matmul(ptot[:, :], ones[:], oh[:, :],
                                             start=True, stop=True,
                                             skip_group_check=True))
            p.wait("v", tm1)
            tv1 = p.op("v", nc.vector.tensor_mul(ohf[:, :], oh[:, :], pcum[:, :]))
            p.wait("v", tv1)
            tv2 = p.op("v", nc.vector.reduce_sum(lp[:, 0:1], ohf[:, :], axis=AX.X))
            if gate_keep is None:
                p.wait("v", tv2)
                tv3 = p.op("v", nc.vector.tensor_single_scalar(
                    ov[:, 0:1], lp[:, 0:1], float(GPAD), ALU.is_ge))
                p.wait("v", tv3)
                tv4 = p.op("v", nc.vector.scalar_tensor_tensor(
                    slot_sb[:, t:t + 1], eidx, float(GPAD), lp[:, 0:1],
                    ALU.mult, ALU.add))
                p.wait("v", tv4)
                tv5 = p.op("v", nc.vector.scalar_tensor_tensor(
                    slot_sb[:, t:t + 1], ov[:, 0:1], OOBV, slot_sb[:, t:t + 1],
                    ALU.mult, ALU.add))
                tend = tv5
            else:
                ownb, capv, gk_sb, gatecol = gate_keep
                p.wait("v", tv2)
                tv3 = p.op("v", nc.vector.tensor_single_scalar(
                    ov[:, 0:1], lp[:, 0:1], capv, ALU.is_lt))
                p.wait("v", tv3)
                tv4 = p.op("v", nc.vector.tensor_mul(
                    gk_sb[:, t:t + 1], ov[:, 0:1],
                    meta_pair_sb[:, c0 + 1:c0 + 2]))
                tend = tv4
            p.wait("v", tm2, tend)
            prev = p.op("v", nc.vector.tensor_add(rb[0:1, :], rb[0:1, :],
                                                  ptot[0:1, :]))
        p.barrier()


def ph_ffn(p, st, xsrc_dram, rows0, ngroups, grp_rows, w13_in, w2_in,
           out_writer, name="", w13row=None, w2row=None, hdump_dram=None):
    nc = p.nc
    ntt = grp_rows // 128
    nsub = [(0, 512), (512, grp_rows - 512)] if grp_rows > 512 else [(0, grp_rows)]
    with ExitStack() as sk:
        xT = p.sb(sk, "ffxT" + name, [128, NDM * grp_rows], BF16)
        w13 = [p.sb(sk, f"ffw{i}" + name, [128, 2 * D], BF16) for i in range(2)]
        w2b = [p.sb(sk, f"ffw2{i}" + name, [128, NHT * 512], BF16) for i in range(2)]
        hsl = p.sb(sk, "ffhs" + name, [128, grp_rows], BF16)
        hh = p.sb(sk, "ffh" + name, [128, NHT * grp_rows], BF16)
        ps1 = [p.psumt(sk, f"ffp1{i}" + name, [128, 512]) for i in range(2)]
        ps2 = [p.psumt(sk, f"ffp2{i}" + name, [128, 512]) for i in range(2)]
        po = [p.psumt(sk, f"ffpo{i}" + name, [128, 512]) for i in range(2)]
        gend = None
        w2free = [None, None]
        for g in range(ngroups):
            tx = None
            p.wait("y", gend)
            for dm in range(NDM):
                tx = p.dma("y", nc.sync.dma_start(
                    out=xT[:, dm * grp_rows:(dm + 1) * grp_rows],
                    in_=xsrc_dram[rows0 + g * grp_rows: rows0 + (g + 1) * grp_rows,
                                  dm * 128:(dm + 1) * 128],
                    transpose=True))
            wfree = [None, None]
            p1free = [None, None]
            p2free = [None, None]
            lastv = None
            for ht in range(NHT):
                wb = ht % 2
                p.wait("p", wfree[wb])
                if w13row is None:
                    r1 = (g * 2 + 0) * NHT + ht
                    r3 = (g * 2 + 1) * NHT + ht
                else:
                    r1 = w13row(g, 0, ht)
                    r3 = w13row(g, 1, ht)
                tw = p.dma("p", nc.gpsimd.dma_start(
                    out=w13[wb][:, 0:D].rearrange("p (t c) -> p t c", c=128),
                    in_=w13_in[r1:r1 + 1, :].rearrange(
                        "o (t p c) -> (o p) t c", p=128, c=128)))
                tw = p.dma("p", nc.gpsimd.dma_start(
                    out=w13[wb][:, D:2 * D].rearrange("p (t c) -> p t c", c=128),
                    in_=w13_in[r3:r3 + 1, :].rearrange(
                        "o (t p c) -> (o p) t c", p=128, c=128)))
                for (s0, w) in nsub:
                    pb = 0 if s0 == 0 else 1
                    p.wait("t", tw, tx if ht == 0 else None, wfree[wb],
                           p1free[pb], p2free[pb])
                    l1 = l3 = None
                    for kt in range(NDM):
                        l1 = p.op("t", nc.tensor.matmul(
                            ps1[pb][:, 0:w],
                            w13[wb][:, kt * 128:(kt + 1) * 128],
                            xT[:, kt * grp_rows + s0: kt * grp_rows + s0 + w],
                            start=(kt == 0), stop=(kt == NDM - 1),
                            skip_group_check=True))
                    for kt in range(NDM):
                        l3 = p.op("t", nc.tensor.matmul(
                            ps2[pb][:, 0:w],
                            w13[wb][:, D + kt * 128: D + (kt + 1) * 128],
                            xT[:, kt * grp_rows + s0: kt * grp_rows + s0 + w],
                            start=(kt == 0), stop=(kt == NDM - 1),
                            skip_group_check=True))
                    p.wait("s", l1)
                    tsl = p.op("s", nc.scalar.activation(
                        hsl[:, s0:s0 + w], ps1[pb][:, 0:w], AF.Silu))
                    p.wait("v", tsl, l3)
                    lastv = p.op("v", nc.vector.tensor_mul(
                        hh[:, ht * grp_rows + s0: ht * grp_rows + s0 + w],
                        ps2[pb][:, 0:w], hsl[:, s0:s0 + w]))
                    p1free[pb] = tsl
                    p2free[pb] = lastv
                wfree[wb] = lastv
            if hdump_dram is not None:
                p.wait("p", lastv)
                p.dma("p", nc.gpsimd.dma_start(out=hdump_dram[:, :], in_=hh[:, :]))
            outdone = None
            pofree = [None, None]
            for dc in range(4):
                wb2 = dc % 2
                p.wait("p", w2free[wb2])
                tw2 = None
                for ht in range(NHT):
                    r2 = ((g * 4 + dc) * NHT + ht) if w2row is None else w2row(g, dc, ht)
                    tw2 = p.dma("p", nc.gpsimd.dma_start(
                        out=w2b[wb2][:, ht * 512:(ht + 1) * 512],
                        in_=w2_in[r2:r2 + 1, :].rearrange(
                            "o (p c) -> (o p) c", p=128)))
                for tt in range(ntt):
                    pb = tt % 2
                    p.wait("t", tw2 if tt == 0 else None,
                           lastv if dc == 0 and tt == 0 else None, pofree[pb])
                    lastmm = None
                    for ht in range(NHT):
                        lastmm = p.op("t", nc.tensor.matmul(
                            po[pb][:, :],
                            hh[:, ht * grp_rows + tt * 128:
                               ht * grp_rows + (tt + 1) * 128],
                            w2b[wb2][:, ht * 512:(ht + 1) * 512],
                            start=(ht == 0), stop=(ht == NHT - 1),
                            skip_group_check=True))
                    tko = out_writer(g, tt, dc, po[pb], lastmm)
                    pofree[pb] = tko
                    outdone = tko
                w2free[wb2] = outdone
            gend = outdone
        p.barrier()


def ph_counts(p, st, eT_sb, counts_sb, bases_sb, base_sel_out, selmask_sb):
    nc = p.nc
    with ExitStack() as sk:
        oh = p.sb(sk, "ctoh", [128, 128], BF16)
        row = p.sb(sk, "ctrow", [1, 128], F32)
        tmp = p.sb(sk, "cttmp", [1, 64], F32)
        pr = p.psumt(sk, "ctpr", [1, 128])
        ones = st["ones_sb"]
        prev = None
        for e in range(E):
            p.wait("v", prev)
            tv1 = p.op("v", nc.vector.tensor_single_scalar(
                oh[:, :], eT_sb[:, :], float(e), ALU.is_equal))
            p.wait("t", tv1, prev)
            tm = p.op("t", nc.tensor.matmul(pr[:, :], ones[:], oh[:, :],
                                            start=True, stop=True,
                                            skip_group_check=True))
            p.wait("v", tm)
            tv2 = p.op("v", nc.vector.tensor_copy(row[0:1, :], pr[0:1, :]))
            p.wait("v", tv2)
            prev = p.op("v", nc.vector.reduce_sum(
                counts_sb[0:1, e * 8:(e + 1) * 8],
                row[0:1, :].rearrange("o (c f) -> o c f", f=16), axis=AX.X))
        p.wait("v", prev)
        tz = p.op("v", nc.vector.memset(bases_sb[0:1, :], 0.0))
        prev = tz
        for c in range(1, 8):
            p.wait("v", prev)
            prev = p.op("v", nc.vector.tensor_add(
                bases_sb[0:1, c:64:8], bases_sb[0:1, c - 1:64:8],
                counts_sb[0:1, c - 1:64:8]))
        p.wait("v", prev)
        t1 = p.op("v", nc.vector.tensor_mul(tmp[0:1, :], bases_sb[0:1, :],
                                            selmask_sb[0:1, :]))
        p.wait("v", t1)
        t2 = p.op("v", nc.vector.reduce_sum(
            base_sel_out[0:1, 0:8],
            tmp[0:1, :].rearrange("o (e c) -> o e c", c=8), axis=AX.X))
        p.barrier()


def switched_dma(p, eng_name, fn):
    g = p.engs[eng_name]
    for arm in g.switch_core_id(n=128):
        half = (arm.logical % 8) % 2
        fn(g, half).then_inc(p._sems[eng_name], 16)
    return p.bump(eng_name, 16)


def ph_dbg_tap(p, taps):
    nc = p.nc
    p.barrier()
    with ExitStack() as sk:
        for (od, sd, nr, ncols, dt) in taps:
            t = p.sb(sk, "dbgt", [128, ncols], dt)
            for i in range(nr // 128):
                tk = p.dma("y", nc.sync.dma_start(
                    out=t[:], in_=sd[i * 128:(i + 1) * 128, :]))
                p.wait("y", tk)
                tk2 = p.dma("y", nc.sync.dma_start(
                    out=od[i * 128:(i + 1) * 128, :], in_=t[:]))
                p.wait("y", tk2)
        p.barrier()


def build_L1(stop=None, dbg=False):
    p = Prog()
    nc = p.nc
    x0f = p.dram("x0f", [S, D], F32, kind="ExternalInput")
    x0o = p.dram("x0o", [2048, D], F32, kind="ExternalInput")
    wqa = p.dram("wqa", [D, 1024], F32, kind="ExternalInput")
    wkvd = p.dram("wkvd", [D, KVR], F32, kind="ExternalInput")
    kvw = p.dram("kvw", [1, KVR], F32, kind="ExternalInput")
    wupk = p.dram("wupk", [KVR, 1024], F32, kind="ExternalInput")
    wupv = p.dram("wupv", [KVR, 1024], F32, kind="ExternalInput")
    woa = p.dram("woa", [D, D], F32, kind="ExternalInput")
    wg = p.dram("wg", [D, 8], F32, kind="ExternalInput")
    w13m = p.dram("w13m", [4 * 2 * NHT, NDM * 128 * 128], F32, kind="ExternalInput")
    w2m = p.dram("w2m", [4 * 4 * NHT, 128 * 512], F32, kind="ExternalInput")
    sw13 = p.dram("sw13", [2 * NHT, NDM * 128 * 128], F32, kind="ExternalInput")
    sw2 = p.dram("sw2", [4 * NHT, 128 * 512], F32, kind="ExternalInput")
    triu_in = p.dram("triu_in", [128, 128], BF16, kind="ExternalInput")
    ones_in = p.dram("ones_in", [128, 1], BF16, kind="ExternalInput")
    ones1x_in = p.dram("ones1x_in", [1, 128], F32, kind="ExternalInput")
    masks_in = p.dram("masks_in", [128, 2048], BF16, kind="ExternalInput")
    salts_in = p.dram("salts_in", [1, 8], I32, kind="ExternalInput")
    fconsts_in = p.dram("fconsts_in", [128, len(CVALS)], F32, kind="ExternalInput")
    meta1_o = p.dram("meta1_o", [128, 96], F32, kind="ExternalOutput")
    oe1_o = p.dram("oe1_o", [WIN, D], BF16, kind="ExternalOutput")
    part2_o = p.dram("part2_o", [2048, D], F32, kind="ExternalOutput")
    xlnA = p.dram("xlnA", [S, D], BF16)
    qtd = p.dram("qtd", [1024, S], BF16)
    ktd = p.dram("ktd", [1024, S], BF16)
    vpd = p.dram("vpd", [4 * S, 256], BF16)
    latd = p.dram("latd", [S, KVR], BF16)
    ctx8 = p.dram("ctx8", [1024, S], BF16)
    ctxm = p.dram("ctxm", [2048, 2048], BF16)
    outT = p.dram("outT", [D, 2048], BF16)
    x1d = p.dram("x1d", [2048, D], F32)
    xln1h = p.dram("xln1h", [2048, D], BF16)
    xln1l = p.dram("xln1l", [2048, D], BF16)
    slotd = p.dram("slotd", [128, 32], F32)
    xdd = p.dram("xdd", [WIN, D], BF16)
    ctx_sh = p.dram("ctx_sh", [2048, S], BF16, addr_space="Shared")
    x2_sh = p.dram("x2_sh", [S, D], F32, addr_space="Shared")
    meta_sh = p.dram("meta_sh", [256, 32], F32, addr_space="Shared")
    xg_sh = p.dram("xg_sh", [2 * WIN, D], BF16, addr_space="Shared")
    flags_sh = p.dram("flags_sh", [2, 8], I32, addr_space="Shared")

    p.open()
    outer = p.ctx
    st = {}
    st["triu_sb"] = p.sb(outer, "triu_sb", [128, 128], BF16)
    st["ones_sb"] = p.sb(outer, "ones_sb", [128, 1], BF16)
    st["ones1x_sb"] = p.sb(outer, "ones1x_sb", [1, 128], F32)
    st["masks_sb"] = p.sb(outer, "masks_sb", [128, 2048], BF16)
    st["onesf_sb"] = p.sb(outer, "onesf_sb", [128, 1], F32)
    salts_sb = p.sb(outer, "salts_sb", [1, 8], I32)
    fc_sb = p.sb(outer, "fc_sb", [128, len(CVALS)], F32)
    kvwb = p.sb(outer, "kvwb", [128, KVR], F32)
    wg_hi = p.sb(outer, "wg_hi", [128, NDM * 8], BF16)
    wg_lo = p.sb(outer, "wg_lo", [128, NDM * 8], BF16)
    meta_eg = p.sb(outer, "meta_eg", [128, 32], F32)
    meta_pair = p.sb(outer, "meta_pair", [128, 64], F32)
    slot_sb = p.sb(outer, "slot_sb", [128, 32], F32)
    myslot = p.sb(outer, "myslot", [128, 16], F32)
    myslot_i = p.sb(outer, "myslot_i", [128, 16], I32)
    zt = p.sb(outer, "zt", [128, D], BF16)
    pst = setup_poll_state(p, outer, flags_sh, salts_sb)

    t1 = p.dma("y", nc.sync.dma_start(out=st["triu_sb"][:], in_=triu_in[:, :]))
    t2 = p.dma("y", nc.sync.dma_start(out=st["ones_sb"][:], in_=ones_in[:, :]))
    t3 = p.dma("y", nc.sync.dma_start(out=st["ones1x_sb"][:], in_=ones1x_in[:, :]))
    t4 = p.dma("y", nc.sync.dma_start(out=st["masks_sb"][:], in_=masks_in[:, :]))
    t5 = p.dma("y", nc.sync.dma_start(out=salts_sb[:], in_=salts_in[:, :]))
    t6 = p.dma("y", nc.sync.dma_start(out=fc_sb[:], in_=fconsts_in[:, :]))
    for i, v in enumerate(CVALS):
        nc.const_aps.register(F32, v, fc_sb[:, i:i + 1])
    p.barrier()
    tz = p.op("v", nc.vector.memset(zt[:], 0.0))
    p.wait("v", t3)
    tof = p.op("v", nc.vector.tensor_copy(st["onesf_sb"][:, 0:1],
                                          st["ones_sb"][:, 0:1]))
    with ExitStack() as sk:
        wgf = p.sb(sk, "wgf", [128, NDM * 8], F32)
        wgh = p.sb(sk, "wgh", [128, NDM * 8], F32)
        tw = None
        for tt in range(NDM):
            tw = p.dma("y", nc.sync.dma_start(
                out=wgf[:, tt * 8:(tt + 1) * 8],
                in_=wg[tt * 128:(tt + 1) * 128, :]))
        p.wait("v", tw)
        a1 = p.op("v", nc.vector.tensor_copy(wg_hi[:, :], wgf[:, :]))
        p.wait("v", a1)
        a2 = p.op("v", nc.vector.tensor_copy(wgh[:, :], wg_hi[:, :]))
        p.wait("v", a2)
        a3 = p.op("v", nc.vector.tensor_sub(wgh[:, :], wgf[:, :], wgh[:, :]))
        p.wait("v", a3)
        a4 = p.op("v", nc.vector.tensor_copy(wg_lo[:, :], wgh[:, :]))
        kvw1 = p.sb(sk, "kvw1", [1, KVR], F32)
        tk1 = p.dma("y", nc.sync.dma_start(out=kvw1[:], in_=kvw[:, :]))
        with ExitStack() as sk2:
            pb = p.psumt(sk2, "kvwp", [128, KVR])
            p.wait("t", tk1, t3)
            tm = p.op("t", nc.tensor.matmul(pb[:, :], st["ones1x_sb"][:],
                                            kvw1[0:1, :], start=True, stop=True,
                                            skip_group_check=True))
            p.wait("s", tm)
            tcp = p.op("s", nc.scalar.copy(kvwb[:], pb[:, :]))
        p.barrier()

    if stop == "consts":
        p.close()
        return p
    for j in range(WIN // 128):
        def zw(g, half, j=j):
            return g.dma_start(
                out=xg_sh[half * WIN + j * 128: half * WIN + (j + 1) * 128, :],
                in_=zt[:])
        p.wait("y", tz)
        switched_dma(p, "y", zw)
    ph_pair_barrier(p, pst, 0)

    if stop == "zero":
        p.close()
        return p
    ph_ln(p, st, x0f, 32, xlnA)

    if stop == "ln0":
        p.close()
        return p
    with ExitStack() as sk:
        wqs = p.sb(sk, "wqs", [128, NDM * 1024], BF16)
        for tt in range(NDM):
            twq = p.dma("p", nc.gpsimd.dma_start(
                out=wqs[:, tt * 1024:(tt + 1) * 1024],
                in_=wqa[tt * 128:(tt + 1) * 128, :]))
        p.wait("t", twq)
        ph_proj_feat(p, st, xlnA, S, wqs, 1024, 8, qtd)
    if stop == "q":
        p.close()
        return p
    with ExitStack() as sk:
        wkd = p.sb(sk, "wkd", [128, NDM * KVR], BF16)
        lst = p.sb(sk, "lst", [128, KVR], BF16)
        ssq = p.sb(sk, "ssq", [128, 1], F32)
        sdmp = p.sb(sk, "sdmp", [128, KVR], BF16)
        evf = {}
        for tt in range(NDM):
            twk = p.dma("p", nc.gpsimd.dma_start(
                out=wkd[:, tt * KVR:(tt + 1) * KVR],
                in_=wkvd[tt * 128:(tt + 1) * 128, :]))
        p.wait("t", twk)

        def lat_writer(ptg, ci, ps_, lastmm):
            p.wait("s", lastmm, evf.get("s"))
            a = p.op("s", nc.scalar.activation(sdmp[:, :], ps_[:, 0:KVR],
                                               AF.Square))
            p.wait("v", a, evf.get("vr"))
            ar = p.op("v", nc.vector.reduce_sum(ssq[:, 0:1], sdmp[:, :], axis=AX.X))
            evf["vr"] = ar
            p.wait("s", ar)
            b = p.op("s", nc.scalar.activation(ssq[:, 0:1], ssq[:, 0:1], AF.Ln,
                                               bias=KVR * 1e-6, scale=1.0))
            p.wait("s", b)
            cc = p.op("s", nc.scalar.activation(ssq[:, 0:1], ssq[:, 0:1], AF.Exp,
                                                scale=-0.5, bias=0.5 * math.log(KVR)))
            p.wait("v", cc, evf.get("v"), evf.get("y"))
            d = p.op("v", nc.vector.scalar_tensor_tensor(
                lst[:, :], ps_[:, 0:KVR], ssq[:, 0:1], kvwb[:, :],
                ALU.mult, ALU.mult))
            p.wait("y", d)
            e_ = p.dma("y", nc.sync.dma_start(
                out=latd[ptg * 128:(ptg + 1) * 128, :], in_=lst[:]))
            evf["s"] = a
            evf["v"] = d
            evf["y"] = e_
            return d
        ph_proj_tok(p, st, xlnA, S, wkd, [(0, KVR)], lat_writer)
    if stop == "lat":
        p.close()
        return p
    with ExitStack() as sk:
        latT = p.sb(sk, "latT", [128, 2 * S], BF16)
        wuk = p.sb(sk, "wuk", [128, 2 * 1024], BF16)
        wuv = p.sb(sk, "wuv", [128, 2 * 1024], BF16)
        for lt in range(2):
            for pc in range(S // 512):
                tl = p.dma("y", nc.sync.dma_start(
                    out=latT[:, lt * S + pc * 512: lt * S + (pc + 1) * 512],
                    in_=latd[pc * 512:(pc + 1) * 512, lt * 128:(lt + 1) * 128],
                    transpose=True))
            tk_ = p.dma("p", nc.gpsimd.dma_start(
                out=wuk[:, lt * 1024:(lt + 1) * 1024],
                in_=wupk[lt * 128:(lt + 1) * 128, :]))
            tv_ = p.dma("p", nc.gpsimd.dma_start(
                out=wuv[:, lt * 1024:(lt + 1) * 1024],
                in_=wupv[lt * 128:(lt + 1) * 128, :]))
        p.wait("t", tl, tk_, tv_)
        ph_proj_feat(p, st, None, S, wuk, 1024, 8, ktd,
                     kacc_sb=latT, kacc_tiles=2, kacc_cols=S)
        vst = p.sb(sk, "vst", [128, 512], BF16)
        vevf = {}

        def v_writer(ptg, ci, ps_, lastmm):
            p.wait("v", lastmm, vevf.get("y"))
            a = p.op("v", nc.vector.tensor_copy(vst[:], ps_[:, :]))
            p.wait("y", a)
            e1 = p.dma("y", nc.sync.dma_start(
                out=vpd[(2 * ci) * S + ptg * 128:(2 * ci) * S + (ptg + 1) * 128, :],
                in_=vst[:, 0:256]))
            e2 = p.dma("y", nc.sync.dma_start(
                out=vpd[(2 * ci + 1) * S + ptg * 128:
                        (2 * ci + 1) * S + (ptg + 1) * 128, :],
                in_=vst[:, 256:512]))
            vevf["y"] = e2
            return a
        ph_proj_tok(p, st, None, S, wuv, [(0, 512), (512, 512)], v_writer,
                    kacc=(latT, 2, S))

    if stop == "kv":
        p.close()
        return p
    def ctxw(h, ctxs_sb):
        return switched_dma(p, "y", lambda g, half: g.dma_start(
            out=ctx_sh[half * 1024 + h * 128: half * 1024 + (h + 1) * 128, :],
            in_=ctxs_sb[:, :]))
    ph_attention(p, st, qtd, ktd, vpd, ctx8, S, ISQ, ctx_writer=ctxw)
    ph_pair_barrier(p, pst, 1)
    with ExitStack() as sk:
        bt = [p.sb(sk, f"cbt{i}", [128, 2048], BF16) for i in range(2)]
        btf = [None, None]
        for at in range(16):
            pa = at % 2
            p.wait("y", btf[pa])
            tin = switched_dma(p, "y", lambda g, half, at=at, pa=pa: g.dma_start(
                out=bt[pa][:, :],
                in_=ctx_sh[at * 128:(at + 1) * 128,
                           half * 2048:(half + 1) * 2048]))
            p.wait("y", tin)
            tko = p.dma("y", nc.sync.dma_start(
                out=ctxm[at * 128:(at + 1) * 128, :], in_=bt[pa][:]))
            btf[pa] = tko
        p.barrier()

    if stop == "attn":
        p.close()
        return p
    ph_wo(p, st, ctxm, woa, outT, 2048)
    ph_ln(p, st, x0o, NTT, xln1h, lo_dram=xln1l, attout_dram=outT, x1_dram=x1d)
    ph_router(p, st, xln1h, xln1l, wg_hi, wg_lo, meta_eg, 2048)

    if stop == "router":
        p.close()
        return p
    tme = switched_dma(p, "y", lambda g, half: g.dma_start(
        out=meta_sh[half * 128:(half + 1) * 128, :], in_=meta_eg[:, :]))
    ph_pair_barrier(p, pst, 2)
    tm1 = p.dma("y", nc.sync.dma_start(out=meta_pair[:, 0:32], in_=meta_sh[0:128, :]))
    tm2 = p.dma("y", nc.sync.dma_start(out=meta_pair[:, 32:64], in_=meta_sh[128:256, :]))
    p.wait("v", tm1, tm2)
    p.wait("t", tm1, tm2)

    def col_of(t):
        return 2 * t if t < 16 else 32 + 2 * (t - 16)
    ph_compact(p, st, meta_pair, slot_sb, 32, col_of=col_of)
    ts1 = p.dma("y", nc.sync.dma_start(out=slotd[:, :], in_=slot_sb[:, :]))
    p.wait("y", ts1)
    ts2 = switched_dma(p, "y", lambda g, half: g.dma_start(
        out=myslot[:, :], in_=slotd[:, half * 16:(half + 1) * 16]))
    p.wait("v", ts2)
    ts3 = p.op("v", nc.vector.tensor_copy(myslot_i[:, :], myslot[:, :]))
    p.wait("p", ts3)
    with ExitStack() as sk:
        sct = [p.sb(sk, f"sct{i}", [128, D], BF16) for i in range(2)]
        sctf = [None, None]
        for j in range(16):
            pa = j % 2
            p.wait("y", sctf[pa])
            tl = p.dma("y", nc.sync.dma_start(
                out=sct[pa][:], in_=xln1h[j * 128:(j + 1) * 128, :]))
            p.wait("p", tl)
            tsc = p.dma("p", nc.gpsimd.indirect_dma_start(
                out=xg_sh[:, :],
                out_offset=IndirectOffsetOnAxis(myslot_i[:, j:j + 1], 0),
                in_=sct[pa][:],
                in_offset=None,
                bounds_check=2 * WIN - 1, oob_is_err=False))
            sctf[pa] = tsc
        p.barrier()
    ph_pair_barrier(p, pst, 3)
    p.dma("y", nc.sync.dma_start(out=meta1_o[:, 0:64], in_=meta_pair[:, :]))
    p.dma("y", nc.sync.dma_start(out=meta1_o[:, 64:96], in_=slot_sb[:, :]))

    if stop == "scatter":
        p.close()
        return p
    with ExitStack() as sk:
        bt2 = [p.sb(sk, f"xdb{i}", [128, D], BF16) for i in range(2)]
        btf2 = [None, None]
        for j in range(WIN // 128):
            pa = j % 2
            p.wait("y", btf2[pa])
            tin = switched_dma(p, "y", lambda g, half, j=j, pa=pa: g.dma_start(
                out=bt2[pa][:, :],
                in_=xg_sh[half * WIN + j * 128: half * WIN + (j + 1) * 128, :]))
            p.wait("y", tin)
            tko = p.dma("y", nc.sync.dma_start(
                out=xdd[j * 128:(j + 1) * 128, :], in_=bt2[pa][:]))
            btf2[pa] = tko
        p.barrier()
    if stop == "xd":
        p.close()
        return p
    with ExitStack() as sk:
        oest = [p.sb(sk, f"oest{i}", [128, 512], BF16) for i in range(2)]
        oevf = [None, None]

        def oe_writer(g, tt, dc, ps_, lastmm):
            pb = (g * 20 + tt * 4 + dc) % 2
            p.wait("v", lastmm, oevf[pb])
            a = p.op("v", nc.vector.tensor_copy(oest[pb][:], ps_[:, :]))
            p.wait("y", a)
            e_ = p.dma("y", nc.sync.dma_start(
                out=oe1_o[g * GPAD + tt * 128: g * GPAD + (tt + 1) * 128,
                          dc * 512:(dc + 1) * 512],
                in_=oest[pb][:]))
            oevf[pb] = e_
            return a
        ph_ffn(p, st, xdd, 0, 4, GPAD, w13m, w2m, oe_writer, name="e")

    if stop == "expert":
        p.close()
        return p
    if dbg:
        dxlnA = p.dram("dbg_xlnA", [S, D], BF16, kind="ExternalOutput")
        dqtd = p.dram("dbg_qtd", [1024, S], BF16, kind="ExternalOutput")
        dktd = p.dram("dbg_ktd", [1024, S], BF16, kind="ExternalOutput")
        dlat = p.dram("dbg_latd", [S, KVR], BF16, kind="ExternalOutput")
        dctx = p.dram("dbg_ctx", [2048, S], BF16, kind="ExternalOutput")
        doutT = p.dram("dbg_outT", [D, 2048], BF16, kind="ExternalOutput")
        dx1 = p.dram("dbg_x1", [2048, D], F32, kind="ExternalOutput")
        dxln1 = p.dram("dbg_xln1", [2048, D], BF16, kind="ExternalOutput")
        ph_dbg_tap(p, [
            (dxlnA, xlnA, S, D, BF16),
            (dqtd, qtd, 1024, S, BF16),
            (dktd, ktd, 1024, S, BF16),
            (dlat, latd, S, KVR, BF16),
            (dctx, ctx_sh, 2048, S, BF16),
            (doutT, outT, D, 2048, BF16),
            (p.dram("dbg_ctxm", [2048, 2048], BF16, kind="ExternalOutput"), ctxm, 2048, 2048, BF16),
            (dx1, x1d, 2048, D, F32),
            (dxln1, xln1h, 2048, D, BF16),
        ])

    with ExitStack() as sk:
        shst = [p.sb(sk, f"shst{i}", [128, 512], F32) for i in range(2)]
        x1c = [p.sb(sk, f"x1c{i}", [128, 512], F32) for i in range(2)]
        shevf = [None, None]

        def sh_writer(g, tt, dc, ps_, lastmm):
            pb = (g * 16 + tt * 4 + dc) % 2
            ttg = g * 4 + tt
            p.wait("y", shevf[pb])
            tx1 = p.dma("y", nc.sync.dma_start(
                out=x1c[pb][:], in_=x1d[ttg * 128:(ttg + 1) * 128,
                                        dc * 512:(dc + 1) * 512]))
            p.wait("v", lastmm, tx1)
            a = p.op("v", nc.vector.tensor_add(shst[pb][:], ps_[:, :], x1c[pb][:]))
            p.wait("y", a)
            e_ = p.dma("y", nc.sync.dma_start(
                out=part2_o[ttg * 128:(ttg + 1) * 128, dc * 512:(dc + 1) * 512],
                in_=shst[pb][:]))
            shevf[pb] = e_
            return a

        def sh_w13row(g, which, ht):
            return which * NHT + ht

        def sh_w2row(g, dc, ht):
            return dc * NHT + ht
        ph_ffn(p, st, xln1h, 0, 4, 512, sw13, sw2, sh_writer, name="s",
               w13row=sh_w13row, w2row=sh_w2row)
    p.close()
    return p


def build_L2(stop=None, dbg=False):
    p = Prog()
    nc = p.nc
    part2o = p.dram("part2o", [2048, D], F32, kind="ExternalInput")
    oe1p = p.dram("oe1p", [2 * WIN, D], BF16, kind="ExternalInput")
    metao = p.dram("metao", [128, 32], F32, kind="ExternalInput")
    slotso = p.dram("slotso", [128, 16], F32, kind="ExternalInput")
    eT_in = p.dram("eT_in", [128, 128], F32, kind="ExternalInput")
    selm = p.dram("selm", [1, 64], F32, kind="ExternalInput")
    wqtb = p.dram("wqtb", [D, 1024], F32, kind="ExternalInput")
    wktb = p.dram("wktb", [D, 1024], F32, kind="ExternalInput")
    wvtb = p.dram("wvtb", [D, 1024], F32, kind="ExternalInput")
    wotb = p.dram("wotb", [D, D], F32, kind="ExternalInput")
    bo_in = p.dram("bo_in", [1, D], F32, kind="ExternalInput")
    cos_in = p.dram("cos_in", [64, S], F32, kind="ExternalInput")
    sin_in = p.dram("sin_in", [64, S], F32, kind="ExternalInput")
    wg = p.dram("wg", [D, 8], F32, kind="ExternalInput")
    w13m = p.dram("w13m", [4 * 2 * NHT, NDM * 128 * 128], F32, kind="ExternalInput")
    w2m = p.dram("w2m", [4 * 4 * NHT, 128 * 512], F32, kind="ExternalInput")
    sw13 = p.dram("sw13", [2 * NHT, NDM * 128 * 128], F32, kind="ExternalInput")
    sw2 = p.dram("sw2", [4 * NHT, 128 * 512], F32, kind="ExternalInput")
    triu_in = p.dram("triu_in", [128, 128], BF16, kind="ExternalInput")
    ones_in = p.dram("ones_in", [128, 1], BF16, kind="ExternalInput")
    ones1x_in = p.dram("ones1x_in", [1, 128], F32, kind="ExternalInput")
    masks_in = p.dram("masks_in", [128, 2048], BF16, kind="ExternalInput")
    salts_in = p.dram("salts_in", [1, 8], I32, kind="ExternalInput")
    fconsts_in = p.dram("fconsts_in", [128, len(CVALS)], F32, kind="ExternalInput")
    meta2_o = p.dram("meta2_o", [128, 96], F32, kind="ExternalOutput")
    oe2_o = p.dram("oe2_o", [WIN, D], BF16, kind="ExternalOutput")
    part4_o = p.dram("part4_o", [2048, D], F32, kind="ExternalOutput")
    x2own = p.dram("x2own", [2048, D], F32)
    xln2 = p.dram("xln2", [S, D], BF16)
    qtd = p.dram("qtd", [1024, S], BF16)
    ktd = p.dram("ktd", [1024, S], BF16)
    vpd = p.dram("vpd", [4 * S, 256], BF16)
    ctx8 = p.dram("ctx8", [1024, S], BF16)
    ctxm = p.dram("ctxm", [2048, 2048], BF16)
    outT = p.dram("outT", [D, 2048], BF16)
    x3d = p.dram("x3d", [2048, D], F32)
    xln3h = p.dram("xln3h", [2048, D], BF16)
    xln3l = p.dram("xln3l", [2048, D], BF16)
    slotd = p.dram("slotd", [128, 32], F32)
    xdd = p.dram("xdd", [WIN, D], BF16)
    ctx_sh = p.dram("ctx_sh", [2048, S], BF16, addr_space="Shared")
    x2_sh = p.dram("x2_sh", [S, D], F32, addr_space="Shared")
    meta_sh = p.dram("meta_sh", [256, 32], F32, addr_space="Shared")
    xg_sh = p.dram("xg_sh", [2 * WIN, D], BF16, addr_space="Shared")
    flags_sh = p.dram("flags_sh", [2, 8], I32, addr_space="Shared")

    p.open()
    outer = p.ctx
    st = {}
    st["triu_sb"] = p.sb(outer, "triu_sb", [128, 128], BF16)
    st["ones_sb"] = p.sb(outer, "ones_sb", [128, 1], BF16)
    st["ones1x_sb"] = p.sb(outer, "ones1x_sb", [1, 128], F32)
    st["masks_sb"] = p.sb(outer, "masks_sb", [128, 2048], BF16)
    salts_sb = p.sb(outer, "salts_sb", [1, 8], I32)
    fc_sb = p.sb(outer, "fc_sb", [128, len(CVALS)], F32)
    wg_hi = p.sb(outer, "wg_hi", [128, NDM * 8], BF16)
    wg_lo = p.sb(outer, "wg_lo", [128, NDM * 8], BF16)
    meta_eg = p.sb(outer, "meta_eg", [128, 32], F32)
    meta_pair = p.sb(outer, "meta_pair", [128, 64], F32)
    slot_sb = p.sb(outer, "slot_sb", [128, 32], F32)
    myslot = p.sb(outer, "myslot", [128, 16], F32)
    myslot_i = p.sb(outer, "myslot_i", [128, 16], I32)
    zt = p.sb(outer, "zt", [128, D], BF16)
    bob = p.sb(outer, "bob", [128, D], F32)
    cosf = p.sb(outer, "cosf", [64, S], F32)
    sinf = p.sb(outer, "sinf", [64, S], F32)
    eT_sb = p.sb(outer, "eT_sb", [128, 128], F32)
    counts_sb = p.sb(outer, "counts_sb", [1, 64], F32)
    bases_sb = p.sb(outer, "bases_sb", [1, 64], F32)
    sel_sb = p.sb(outer, "sel_sb", [1, 64], F32)
    ownb0 = p.sb(outer, "ownb0", [1, 8], F32)
    ownb1 = p.sb(outer, "ownb1", [1, 8], F32)
    gk_sb = p.sb(outer, "gk_sb", [128, 32], F32)
    pst = setup_poll_state(p, outer, flags_sh, salts_sb)

    p.dma("y", nc.sync.dma_start(out=st["triu_sb"][:], in_=triu_in[:, :]))
    p.dma("y", nc.sync.dma_start(out=st["ones_sb"][:], in_=ones_in[:, :]))
    p.dma("y", nc.sync.dma_start(out=st["ones1x_sb"][:], in_=ones1x_in[:, :]))
    p.dma("y", nc.sync.dma_start(out=st["masks_sb"][:], in_=masks_in[:, :]))
    p.dma("y", nc.sync.dma_start(out=salts_sb[:], in_=salts_in[:, :]))
    p.dma("y", nc.sync.dma_start(out=fc_sb[:], in_=fconsts_in[:, :]))
    for i, v in enumerate(CVALS):
        nc.const_aps.register(F32, v, fc_sb[:, i:i + 1])
    p.dma("y", nc.sync.dma_start(out=bob[:], in_=bo_in[:, :].partition_broadcast(128)))
    p.dma("y", nc.sync.dma_start(out=cosf[:], in_=cos_in[:, :]))
    p.dma("y", nc.sync.dma_start(out=sinf[:], in_=sin_in[:, :]))
    p.dma("y", nc.sync.dma_start(out=eT_sb[:], in_=eT_in[:, :]))
    p.dma("y", nc.sync.dma_start(out=sel_sb[:], in_=selm[:, :]))
    meta_own_sb = p.sb(outer, "meta_own_sb", [128, 32], F32)
    slots_own_sb = p.sb(outer, "slots_own_sb", [128, 16], F32)
    p.dma("y", nc.sync.dma_start(out=meta_own_sb[:, :], in_=metao[:, :]))
    p.dma("y", nc.sync.dma_start(out=slots_own_sb[:, :], in_=slotso[:, :]))
    tz = p.op("v", nc.vector.memset(zt[:], 0.0))
    with ExitStack() as sk:
        wgf = p.sb(sk, "wgf", [128, NDM * 8], F32)
        wgh = p.sb(sk, "wgh", [128, NDM * 8], F32)
        tw = None
        for tt in range(NDM):
            tw = p.dma("y", nc.sync.dma_start(
                out=wgf[:, tt * 8:(tt + 1) * 8],
                in_=wg[tt * 128:(tt + 1) * 128, :]))
        p.wait("v", tw)
        a1 = p.op("v", nc.vector.tensor_copy(wg_hi[:, :], wgf[:, :]))
        p.wait("v", a1)
        a2 = p.op("v", nc.vector.tensor_copy(wgh[:, :], wg_hi[:, :]))
        p.wait("v", a2)
        a3 = p.op("v", nc.vector.tensor_sub(wgh[:, :], wgf[:, :], wgh[:, :]))
        p.wait("v", a3)
        a4 = p.op("v", nc.vector.tensor_copy(wg_lo[:, :], wgh[:, :]))
        p.barrier()

    for j in range(WIN // 128):
        def zw(g, half, j=j):
            return g.dma_start(
                out=xg_sh[half * WIN + j * 128: half * WIN + (j + 1) * 128, :],
                in_=zt[:])
        p.wait("y", tz)
        switched_dma(p, "y", zw)
    ph_pair_barrier(p, pst, 0)
    if stop == "consts":
        p.close()
        return p

    ph_counts(p, st, eT_sb, counts_sb, bases_sb, ownb0, sel_sb)
    ph_compact(p, st, meta_own_sb, None, 16, base_init_sb=ownb0,
               gate_keep=(ownb0, 2048.0, gk_sb, None))
    if stop == "gk":
        p.close()
        return p

    def col_of(t):
        return 2 * t if t < 16 else 32 + 2 * (t - 16)

    with ExitStack() as sk:
        slt_i = p.sb(sk, "slt_i", [128, 16], I32)
        gth = [p.sb(sk, f"gth{i}", [128, D], BF16) for i in range(2)]
        p2t = [p.sb(sk, f"p2t{i}", [128, D], F32) for i in range(2)]
        x2t = [p.sb(sk, f"x2t{i}", [128, D], F32) for i in range(2)]
        tci = p.op("v", nc.vector.tensor_copy(slt_i[:, :], slots_own_sb[:, :]))
        gfree = [None, None]
        for t in range(NTT):
            pa = t % 2
            p.wait("p", gfree[pa], tci if t < 2 else None)
            tm0 = p.op("p", nc.gpsimd.memset(gth[pa][:], 0.0))
            p.wait("p", tm0)
            tg = p.dma("p", nc.gpsimd.indirect_dma_start(
                out=gth[pa][:],
                out_offset=None,
                in_=oe1p[:, :],
                in_offset=IndirectOffsetOnAxis(slt_i[:, t:t + 1], 0),
                bounds_check=2 * WIN - 1, oob_is_err=False))
            p.wait("y", gfree[pa])
            tp2 = p.dma("y", nc.sync.dma_start(
                out=p2t[pa][:], in_=part2o[t * 128:(t + 1) * 128, :]))
            p.wait("v", tg, tp2)
            tx2 = p.op("v", nc.vector.scalar_tensor_tensor(
                x2t[pa][:], gth[pa][:], gk_sb[:, t:t + 1], p2t[pa][:],
                ALU.mult, ALU.add))
            p.wait("y", tx2)
            ts_ = p.dma("y", nc.sync.dma_start(
                out=x2own[t * 128:(t + 1) * 128, :], in_=x2t[pa][:]))
            tsh = switched_dma(p, "y", lambda g, half, t=t, pa=pa: g.dma_start(
                out=x2_sh[half * 2048 + t * 128: half * 2048 + (t + 1) * 128, :],
                in_=x2t[pa][:, :]))
            gfree[pa] = tsh
        p.barrier()
    ph_pair_barrier(p, pst, 4)
    if stop == "x2":
        p.close()
        return p

    ph_ln(p, st, x2_sh, 32, xln2)

    with ExitStack() as sk:
        wqs = p.sb(sk, "wqs", [128, NDM * 1024], BF16)
        for tt in range(NDM):
            twq = p.dma("p", nc.gpsimd.dma_start(
                out=wqs[:, tt * 1024:(tt + 1) * 1024],
                in_=wqtb[tt * 128:(tt + 1) * 128, :]))
        p.wait("t", twq)
        ph_proj_feat(p, st, xln2, S, wqs, 1024, 8, qtd, rope=True,
                     cosf=cosf, sinf=sinf)
    with ExitStack() as sk:
        wks = p.sb(sk, "wks", [128, NDM * 1024], BF16)
        for tt in range(NDM):
            twk = p.dma("p", nc.gpsimd.dma_start(
                out=wks[:, tt * 1024:(tt + 1) * 1024],
                in_=wktb[tt * 128:(tt + 1) * 128, :]))
        p.wait("t", twk)
        ph_proj_feat(p, st, xln2, S, wks, 1024, 8, ktd, rope=True,
                     cosf=cosf, sinf=sinf)
    with ExitStack() as sk:
        wvs = p.sb(sk, "wvs", [128, NDM * 1024], BF16)
        for tt in range(NDM):
            twv = p.dma("p", nc.gpsimd.dma_start(
                out=wvs[:, tt * 1024:(tt + 1) * 1024],
                in_=wvtb[tt * 128:(tt + 1) * 128, :]))
        p.wait("t", twv)
        vst = p.sb(sk, "vst", [128, 512], BF16)
        vevf = {}

        def v_writer(ptg, ci, ps_, lastmm):
            p.wait("v", lastmm, vevf.get("y"))
            a = p.op("v", nc.vector.tensor_copy(vst[:], ps_[:, :]))
            p.wait("y", a)
            e1 = p.dma("y", nc.sync.dma_start(
                out=vpd[(2 * ci) * S + ptg * 128:(2 * ci) * S + (ptg + 1) * 128, :],
                in_=vst[:, 0:256]))
            e2 = p.dma("y", nc.sync.dma_start(
                out=vpd[(2 * ci + 1) * S + ptg * 128:
                        (2 * ci + 1) * S + (ptg + 1) * 128, :],
                in_=vst[:, 256:512]))
            vevf["y"] = e2
            return a
        ph_proj_tok(p, st, xln2, S, wvs, [(0, 512), (512, 512)], v_writer)
    if stop == "proj":
        p.close()
        return p

    def ctxw(h, ctxs_sb):
        return switched_dma(p, "y", lambda g, half: g.dma_start(
            out=ctx_sh[half * 1024 + h * 128: half * 1024 + (h + 1) * 128, :],
            in_=ctxs_sb[:, :]))
    ph_attention(p, st, qtd, ktd, vpd, ctx8, S, ISQ, ctx_writer=ctxw)
    ph_pair_barrier(p, pst, 1)
    with ExitStack() as sk:
        bt = [p.sb(sk, f"cbt{i}", [128, 2048], BF16) for i in range(2)]
        btf = [None, None]
        for at in range(16):
            pa = at % 2
            p.wait("y", btf[pa])
            tin = switched_dma(p, "y", lambda g, half, at=at, pa=pa: g.dma_start(
                out=bt[pa][:, :],
                in_=ctx_sh[at * 128:(at + 1) * 128,
                           half * 2048:(half + 1) * 2048]))
            p.wait("y", tin)
            tko = p.dma("y", nc.sync.dma_start(
                out=ctxm[at * 128:(at + 1) * 128, :], in_=bt[pa][:]))
            btf[pa] = tko
        p.barrier()
    ph_wo(p, st, ctxm, wotb, outT, 2048)
    if stop == "wo":
        p.close()
        return p

    ph_ln(p, st, x2own, NTT, xln3h, lo_dram=xln3l, attout_dram=outT,
          x1_dram=x3d, bob_sb=bob)
    ph_router(p, st, xln3h, xln3l, wg_hi, wg_lo, meta_eg, 2048)
    if stop == "router":
        p.close()
        return p

    switched_dma(p, "y", lambda g, half: g.dma_start(
        out=meta_sh[half * 128:(half + 1) * 128, :], in_=meta_eg[:, :]))
    ph_pair_barrier(p, pst, 2)
    tm1 = p.dma("y", nc.sync.dma_start(out=meta_pair[:, 0:32], in_=meta_sh[0:128, :]))
    tm2 = p.dma("y", nc.sync.dma_start(out=meta_pair[:, 32:64], in_=meta_sh[128:256, :]))
    p.wait("v", tm1, tm2)
    p.wait("t", tm1, tm2)
    ph_compact(p, st, meta_pair, slot_sb, 32, col_of=col_of)
    ts1 = p.dma("y", nc.sync.dma_start(out=slotd[:, :], in_=slot_sb[:, :]))
    p.wait("y", ts1)
    ts2 = switched_dma(p, "y", lambda g, half: g.dma_start(
        out=myslot[:, :], in_=slotd[:, half * 16:(half + 1) * 16]))
    p.wait("v", ts2)
    ts3 = p.op("v", nc.vector.tensor_copy(myslot_i[:, :], myslot[:, :]))
    p.wait("p", ts3)
    with ExitStack() as sk:
        sct = [p.sb(sk, f"sct{i}", [128, D], BF16) for i in range(2)]
        sctf = [None, None]
        for j in range(16):
            pa = j % 2
            p.wait("y", sctf[pa])
            tl = p.dma("y", nc.sync.dma_start(
                out=sct[pa][:], in_=xln3h[j * 128:(j + 1) * 128, :]))
            p.wait("p", tl)
            tsc = p.dma("p", nc.gpsimd.indirect_dma_start(
                out=xg_sh[:, :],
                out_offset=IndirectOffsetOnAxis(myslot_i[:, j:j + 1], 0),
                in_=sct[pa][:],
                in_offset=None,
                bounds_check=2 * WIN - 1, oob_is_err=False))
            sctf[pa] = tsc
        p.barrier()
    ph_pair_barrier(p, pst, 3)
    p.dma("y", nc.sync.dma_start(out=meta2_o[:, 0:64], in_=meta_pair[:, :]))
    p.dma("y", nc.sync.dma_start(out=meta2_o[:, 64:96], in_=slot_sb[:, :]))

    with ExitStack() as sk:
        bt2 = [p.sb(sk, f"xdb{i}", [128, D], BF16) for i in range(2)]
        btf2 = [None, None]
        for j in range(WIN // 128):
            pa = j % 2
            p.wait("y", btf2[pa])
            tin = switched_dma(p, "y", lambda g, half, j=j, pa=pa: g.dma_start(
                out=bt2[pa][:, :],
                in_=xg_sh[half * WIN + j * 128: half * WIN + (j + 1) * 128, :]))
            p.wait("y", tin)
            tko = p.dma("y", nc.sync.dma_start(
                out=xdd[j * 128:(j + 1) * 128, :], in_=bt2[pa][:]))
            btf2[pa] = tko
        p.barrier()
    with ExitStack() as sk:
        oest = [p.sb(sk, f"oest{i}", [128, 512], BF16) for i in range(2)]
        oevf = [None, None]

        def oe_writer(g, tt, dc, ps_, lastmm):
            pb = (g * 20 + tt * 4 + dc) % 2
            p.wait("v", lastmm, oevf[pb])
            a = p.op("v", nc.vector.tensor_copy(oest[pb][:], ps_[:, :]))
            p.wait("y", a)
            e_ = p.dma("y", nc.sync.dma_start(
                out=oe2_o[g * GPAD + tt * 128: g * GPAD + (tt + 1) * 128,
                          dc * 512:(dc + 1) * 512],
                in_=oest[pb][:]))
            oevf[pb] = e_
            return a
        ph_ffn(p, st, xdd, 0, 4, GPAD, w13m, w2m, oe_writer, name="e")

    with ExitStack() as sk:
        shst = [p.sb(sk, f"shst{i}", [128, 512], F32) for i in range(2)]
        x3c = [p.sb(sk, f"x3c{i}", [128, 512], F32) for i in range(2)]
        shevf = [None, None]

        def sh_writer(g, tt, dc, ps_, lastmm):
            pb = (g * 16 + tt * 4 + dc) % 2
            ttg = g * 4 + tt
            p.wait("y", shevf[pb])
            tx1 = p.dma("y", nc.sync.dma_start(
                out=x3c[pb][:], in_=x3d[ttg * 128:(ttg + 1) * 128,
                                        dc * 512:(dc + 1) * 512]))
            p.wait("v", lastmm, tx1)
            a = p.op("v", nc.vector.tensor_add(shst[pb][:], ps_[:, :], x3c[pb][:]))
            p.wait("y", a)
            e_ = p.dma("y", nc.sync.dma_start(
                out=part4_o[ttg * 128:(ttg + 1) * 128, dc * 512:(dc + 1) * 512],
                in_=shst[pb][:]))
            shevf[pb] = e_
            return a

        def sh_w13row(g, which, ht):
            return which * NHT + ht

        def sh_w2row(g, dc, ht):
            return dc * NHT + ht
        ph_ffn(p, st, xln3h, 0, 4, 512, sw13, sw2, sh_writer, name="s",
               w13row=sh_w13row, w2row=sh_w2row)
    p.close()
    return p


def build_L3():
    p = Prog()
    nc = p.nc
    part4 = p.dram("part4", [2048, D], F32, kind="ExternalInput")
    oe2p = p.dram("oe2p", [2 * WIN, D], BF16, kind="ExternalInput")
    meta_own = p.dram("meta_own", [128, 32], F32, kind="ExternalInput")
    slots_own = p.dram("slots_own", [128, 16], F32, kind="ExternalInput")
    eT_in = p.dram("eT_in", [128, 128], F32, kind="ExternalInput")
    sel_in = p.dram("sel_in", [1, 64], F32, kind="ExternalInput")
    triu_in = p.dram("triu_in", [128, 128], BF16, kind="ExternalInput")
    ones_in = p.dram("ones_in", [128, 1], BF16, kind="ExternalInput")
    ones1x_in = p.dram("ones1x_in", [1, 128], F32, kind="ExternalInput")
    fconsts_in = p.dram("fconsts_in", [128, len(CVALS)], F32, kind="ExternalInput")
    xout = p.dram("xout", [2048, D], F32, kind="ExternalOutput")

    p.open()
    outer = p.ctx
    st = {}
    st["triu_sb"] = p.sb(outer, "triu_sb", [128, 128], BF16)
    st["ones_sb"] = p.sb(outer, "ones_sb", [128, 1], BF16)
    st["ones1x_sb"] = p.sb(outer, "ones1x_sb", [1, 128], F32)
    fc_sb = p.sb(outer, "fc_sb", [128, len(CVALS)], F32)
    eT_sb = p.sb(outer, "eT_sb", [128, 128], F32)
    counts_sb = p.sb(outer, "counts_sb", [1, 64], F32)
    bases_sb = p.sb(outer, "bases_sb", [1, 64], F32)
    sel_sb = p.sb(outer, "sel_sb", [1, 64], F32)
    ownb = p.sb(outer, "ownb", [1, 8], F32)
    gk_sb = p.sb(outer, "gk_sb", [128, 16], F32)
    meta_own_sb = p.sb(outer, "meta_own_sb", [128, 32], F32)
    slots_sb = p.sb(outer, "slots_sb", [128, 16], F32)
    slots_i = p.sb(outer, "slots_i", [128, 16], I32)

    p.dma("y", nc.sync.dma_start(out=st["triu_sb"][:], in_=triu_in[:, :]))
    p.dma("y", nc.sync.dma_start(out=st["ones_sb"][:], in_=ones_in[:, :]))
    p.dma("y", nc.sync.dma_start(out=st["ones1x_sb"][:], in_=ones1x_in[:, :]))
    p.dma("y", nc.sync.dma_start(out=fc_sb[:], in_=fconsts_in[:, :]))
    for i, v in enumerate(CVALS):
        nc.const_aps.register(F32, v, fc_sb[:, i:i + 1])
    p.dma("y", nc.sync.dma_start(out=eT_sb[:], in_=eT_in[:, :]))
    p.dma("y", nc.sync.dma_start(out=sel_sb[:], in_=sel_in[:, :]))
    p.dma("y", nc.sync.dma_start(out=meta_own_sb[:, :], in_=meta_own[:, :]))
    tsl = p.dma("y", nc.sync.dma_start(out=slots_sb[:, :], in_=slots_own[:, :]))
    p.barrier()
    ph_counts(p, st, eT_sb, counts_sb, bases_sb, ownb, sel_sb)
    ph_compact(p, st, meta_own_sb, None, 16, base_init_sb=ownb,
               gate_keep=(ownb, 2048.0, gk_sb, None))
    tci = p.op("v", nc.vector.tensor_copy(slots_i[:, :], slots_sb[:, :]))
    with ExitStack() as sk:
        gth = [p.sb(sk, f"gth{i}", [128, D], BF16) for i in range(2)]
        p4t = [p.sb(sk, f"p4t{i}", [128, D], F32) for i in range(2)]
        xot = [p.sb(sk, f"xot{i}", [128, D], F32) for i in range(2)]
        gfree = [None, None]
        for t in range(NTT):
            pa = t % 2
            p.wait("p", gfree[pa], tci if t < 2 else None)
            tm0 = p.op("p", nc.gpsimd.memset(gth[pa][:], 0.0))
            p.wait("p", tm0)
            tg = p.dma("p", nc.gpsimd.indirect_dma_start(
                out=gth[pa][:],
                out_offset=None,
                in_=oe2p[:, :],
                in_offset=IndirectOffsetOnAxis(slots_i[:, t:t + 1], 0),
                bounds_check=2 * WIN - 1, oob_is_err=False))
            p.wait("y", gfree[pa])
            tp4 = p.dma("y", nc.sync.dma_start(
                out=p4t[pa][:], in_=part4[t * 128:(t + 1) * 128, :]))
            p.wait("v", tg, tp4)
            tx = p.op("v", nc.vector.scalar_tensor_tensor(
                xot[pa][:], gth[pa][:], gk_sb[:, t:t + 1], p4t[pa][:],
                ALU.mult, ALU.add))
            p.wait("y", tx)
            ts_ = p.dma("y", nc.sync.dma_start(
                out=xout[t * 128:(t + 1) * 128, :], in_=xot[pa][:]))
            gfree[pa] = ts_
        p.barrier()
    p.close()
    return p


import os

def tile_w13(W):
    return np.ascontiguousarray(
        W.reshape(16, 128, 22, 128).transpose(2, 0, 1, 3).reshape(22, -1))


def tile_w2(W):
    return np.ascontiguousarray(
        W.reshape(22, 128, 4, 512).transpose(2, 0, 1, 3).reshape(88, -1))


def fconsts_np(cvals):
    return np.ascontiguousarray(
        np.tile(np.asarray(cvals, np.float32)[None, :], (128, 1)))


def consts_np():
    c = {}
    c["triu_in"] = np.ascontiguousarray(
        np.triu(np.ones((128, 128), np.float32), k=1).astype(bfd))
    c["ones_in"] = np.ones((128, 1), np.float32).astype(bfd)
    c["ones1x_in"] = np.ones((1, 128), np.float32)
    masks = []
    for dlt in (0, 128, 256, 384):
        kk = np.arange(128)[:, None] + dlt
        qq = np.arange(512)[None, :]
        masks.append((kk <= qq).astype(np.float32))
    c["masks_in"] = np.ascontiguousarray(np.concatenate(masks, 1).astype(bfd))
    return c


def salts_np():
    v = np.frombuffer(os.urandom(32), dtype=np.int32).copy()
    v = np.abs(v) % 1000000007 + 7
    return v.reshape(1, 8)


def prep_L1(inputs):
    cs = consts_np()
    x = np.asarray(inputs["L0_attn_a_x"], np.float32)
    wq = np.asarray(inputs["L0_attn_a_Wq"], np.float32)
    wkvd = np.asarray(inputs["L0_attn_a_Wkv_down"], np.float32)
    kvw = np.asarray(inputs["L0_attn_a_kv_norm_w"], np.float32)
    wup = np.asarray(inputs["L0_attn_a_Wkv_up"], np.float32)
    wo = np.asarray(inputs["L0_attn_a_Wo"], np.float32)
    wg = np.asarray(inputs["L0_ffn_Wg"], np.float32)
    W1 = np.asarray(inputs["L0_ffn_W1"], np.float32)
    W3 = np.asarray(inputs["L0_ffn_W3"], np.float32)
    W2 = np.asarray(inputs["L0_ffn_W2"], np.float32)
    sw1 = np.asarray(inputs["L0_ffn_sw1"], np.float32)
    sw3 = np.asarray(inputs["L0_ffn_sw3"], np.float32)
    sw2 = np.asarray(inputs["L0_ffn_sw2"], np.float32)
    sw13 = np.concatenate([tile_w13(sw1), tile_w13(sw3)], 0)
    sw2t = tile_w2(sw2)
    salts = salts_np()
    fc = fconsts_np(CVALS)
    maps = []
    for c in range(8):
        b, half = c // 2, c % 2
        e0 = half * 4
        w13m = np.concatenate(
            sum([[tile_w13(W1[e0 + g]), tile_w13(W3[e0 + g])] for g in range(4)],
                []), 0)
        w2m = np.concatenate([tile_w2(W2[e0 + g]) for g in range(4)], 0)
        m = {
            "x0f": np.ascontiguousarray(x[b]),
            "x0o": np.ascontiguousarray(x[b, half * 2048:(half + 1) * 2048]),
            "wqa": np.ascontiguousarray(wq[:, half * 1024:(half + 1) * 1024]),
            "wkvd": wkvd,
            "kvw": np.ascontiguousarray(kvw[None, :]),
            "wupk": np.ascontiguousarray(wup[:, half * 1024:(half + 1) * 1024]),
            "wupv": np.ascontiguousarray(wup[:, 2048 + half * 1024:
                                             2048 + (half + 1) * 1024]),
            "woa": wo,
            "wg": wg,
            "w13m": w13m,
            "w2m": w2m,
            "sw13": sw13,
            "sw2": sw2t,
            "salts_in": salts,
            "fconsts_in": fc,
        }
        m.update(cs)
        maps.append(m)
    return maps


def rope_perm():
    return np.concatenate([np.arange(0, 128, 2), np.arange(1, 128, 2)])


def col_of(t):
    return 2 * t if t < 16 else 32 + 2 * (t - 16)


def eT_from_meta(r, key="meta1_o"):
    eT = np.zeros((128, 128), np.float32)
    for cc in range(8):
        pairb = cc // 2
        halfc = cc % 2
        meta = r[2 * pairb]["meta1_o" if key == "meta1_o" else key]
        for f in range(16):
            t = 16 * halfc + f
            eT[:, cc * 16 + f] = meta[:, col_of(t)]
    return np.ascontiguousarray(eT)


def selmask_np(core):
    m = np.zeros((1, 64), np.float32)
    for e in range(8):
        m[0, e * 8 + core] = 1.0
    return m


def prep_L2(inputs, r1):
    cs = consts_np()
    fc = fconsts_np(CVALS)
    wqb = np.asarray(inputs["L0_attn_b_Wq"], np.float32)
    wkb = np.asarray(inputs["L0_attn_b_Wk"], np.float32)
    wvb = np.asarray(inputs["L0_attn_b_Wv"], np.float32)
    wob = np.asarray(inputs["L0_attn_b_Wo"], np.float32)
    bo = np.asarray(inputs["L0_attn_b_bo"], np.float32)
    wg = np.asarray(inputs["L0_ffn_Wg"], np.float32)
    W1 = np.asarray(inputs["L0_ffn_W1"], np.float32)
    W3 = np.asarray(inputs["L0_ffn_W3"], np.float32)
    W2 = np.asarray(inputs["L0_ffn_W2"], np.float32)
    sw1 = np.asarray(inputs["L0_ffn_sw1"], np.float32)
    sw3 = np.asarray(inputs["L0_ffn_sw3"], np.float32)
    sw2 = np.asarray(inputs["L0_ffn_sw2"], np.float32)
    sw13 = np.concatenate([tile_w13(sw1), tile_w13(sw3)], 0)
    sw2t = tile_w2(sw2)
    inv = 1.0 / (10000.0 ** (np.arange(0, 128, 2, dtype=np.float32) / 128.0))
    ang = np.arange(4096, dtype=np.float32)[None, :] * inv[:, None]
    cosf = np.ascontiguousarray(np.cos(ang).astype(np.float32))
    sinf = np.ascontiguousarray(np.sin(ang).astype(np.float32))
    eT = eT_from_meta(r1)
    salts = salts_np()
    perm = rope_perm()
    maps = []
    for c in range(8):
        b, half = c // 2, c % 2
        e0 = half * 4
        meta1 = r1[2 * b]["meta1_o"]
        def headT(W, do_perm):
            rows = []
            for h in range(half * 8, half * 8 + 8):
                blk = W[h * 128:(h + 1) * 128, :]
                if do_perm:
                    blk = blk[perm]
                rows.append(blk)
            return np.ascontiguousarray(np.concatenate(rows, 0).T)
        w13m = np.concatenate(
            sum([[tile_w13(W1[e0 + g]), tile_w13(W3[e0 + g])] for g in range(4)],
                []), 0)
        w2m = np.concatenate([tile_w2(W2[e0 + g]) for g in range(4)], 0)
        m = {
            "part2o": np.ascontiguousarray(r1[c]["part2_o"]),
            "oe1p": np.ascontiguousarray(
                np.concatenate([r1[2 * b]["oe1_o"], r1[2 * b + 1]["oe1_o"]], 0)),
            "metao": np.ascontiguousarray(meta1[:, 32 * half:32 * half + 32]),
            "slotso": np.ascontiguousarray(
                meta1[:, 64 + 16 * half:64 + 16 * half + 16]),
            "eT_in": eT,
            "selm": selmask_np(c),
            "wqtb": headT(wqb, True),
            "wktb": headT(wkb, True),
            "wvtb": headT(wvb, False),
            "wotb": np.ascontiguousarray(wob.T),
            "bo_in": np.ascontiguousarray(bo[None, :]),
            "cos_in": cosf,
            "sin_in": sinf,
            "wg": wg,
            "w13m": w13m,
            "w2m": w2m,
            "sw13": sw13,
            "sw2": sw2t,
            "salts_in": salts,
            "fconsts_in": fc,
        }
        m.update(cs)
        maps.append(m)
    return maps


def prep_L3(r2):
    cs = consts_np()
    fc = fconsts_np(CVALS)
    eT = eT_from_meta(r2, key="meta2_o")
    maps = []
    for c in range(8):
        b, half = c // 2, c % 2
        meta2 = r2[2 * b]["meta2_o"]
        m = {
            "part4": np.ascontiguousarray(r2[c]["part4_o"]),
            "oe2p": np.ascontiguousarray(
                np.concatenate([r2[2 * b]["oe2_o"], r2[2 * b + 1]["oe2_o"]], 0)),
            "meta_own": np.ascontiguousarray(meta2[:, 32 * half:32 * half + 32]),
            "slots_own": np.ascontiguousarray(
                meta2[:, 64 + 16 * half:64 + 16 * half + 16]),
            "eT_in": eT,
            "sel_in": selmask_np(c),
            "triu_in": cs["triu_in"],
            "ones_in": cs["ones_in"],
            "ones1x_in": cs["ones1x_in"],
            "fconsts_in": fc,
        }
        maps.append(m)
    return maps


def kernel(**inputs):
    from concourse.bass_utils import run_bass_kernel_spmd
    inputs = {k: np.asarray(v) for k, v in inputs.items()}
    p1 = build_L1()
    m1 = prep_L1(inputs)
    r1 = run_bass_kernel_spmd(p1.nc, m1, core_ids=list(range(8))).results
    p2 = build_L2()
    m2 = prep_L2(inputs, r1)
    r2 = run_bass_kernel_spmd(p2.nc, m2, core_ids=list(range(8))).results
    p3 = build_L3()
    m3 = prep_L3(r2)
    r3 = run_bass_kernel_spmd(p3.nc, m3, core_ids=list(range(8))).results
    out = np.zeros((4, 4096, 2048), np.float32)
    for c in range(8):
        b, half = c // 2, c % 2
        out[b, half * 2048:(half + 1) * 2048] = r3[c]["xout"]
    return out
